# version 09
# baseline (speedup 1.0000x reference)
import numpy as np
import ml_dtypes

import concourse.bass as bass
import concourse.mybir as mybir
from concourse.bass_utils import run_bass_kernel_spmd

F32 = mybir.dt.float32
BF16 = mybir.dt.bfloat16
I32 = mybir.dt.int32
AF = mybir.ActivationFunctionType
ALU = mybir.AluOpType
AXX = mybir.AxisListType.X
TENSOR_DRAIN = [True]
NO_DIV = [False]
EXP_SPLIT = [False]

T = 4096
TQ = 2048
E = 1024
H = 16
FH = 3072
VROW = H * 65
EPS = 1e-5
SBUF_BASE = 16384 + 2048
SBUF_LIMIT = 229184


class Sem:
    def __init__(self, name):
        self.name = name
        self.n = 0
        self.h = None


class DmaQ:

    def __init__(self, name, nlanes):
        self.lanes = [Sem("%s%d" % (name, i)) for i in range(nlanes)]
        self.i = 0

    def next_tag(self):
        lane = self.lanes[self.i % len(self.lanes)]
        self.i += 1
        lane.n += 16
        return (lane, lane.n)

    def all_tags(self):
        return [(l, l.n) for l in self.lanes]


class Prog:
    def __init__(self):
        self.ops = []

    def op(self, eng, fn, waits=(), inc=None, amt=1, ndma=None, dr=False):
        flat = []
        for w in waits:
            if w is None:
                continue
            if isinstance(w, list):
                flat.extend(x for x in w if x is not None)
            else:
                flat.append(w)
        waits = flat
        if inc is None:
            self.ops.append((eng, fn, waits, None))
            return None
        if ndma is not None:
            inc.n += 16 * ndma
            self.ops.append((eng, fn, waits, (inc, 16, True, dr)))
        else:
            inc.n += amt
            self.ops.append((eng, fn, waits, (inc, amt, False, dr)))
        return (inc, inc.n)

    def emit(self, eng_name, eng):
        last_wait = {}
        for e, fn, waits, inc in self.ops:
            if e != eng_name:
                continue
            for (sem, val) in waits:
                if last_wait.get(sem.name, -1) >= val:
                    continue
                eng.wait_ge(sem.h, val)
                last_wait[sem.name] = val
            inst = fn(eng)
            if inc is not None:
                sem, amt, each, dr = inc
                insts = inst if isinstance(inst, list) else [inst]
                if each:
                    for i in insts:
                        i.then_inc(sem.h, amt)
                elif dr:
                    eng.drain().then_inc(sem.h, amt)
                else:
                    insts[-1].then_inc(sem.h, amt)


def build_program(debug_y=False, paranoid=0):
    nc = bass.Bass("TRN2", target_bir_lowering=False)

    x_in = nc.dram_tensor("x_local", [T, E], F32, kind="ExternalInput")
    wqkv_in = nc.dram_tensor("wqkv_t", [E, 3 * E], F32, kind="ExternalInput")
    wout_in = nc.dram_tensor("wout_t", [E, E], F32, kind="ExternalInput")
    wg_in = nc.dram_tensor("wg_t", [E, FH], I32, kind="ExternalInput")
    wu_in = nc.dram_tensor("wu_t", [E, FH], I32, kind="ExternalInput")
    wd_in = nc.dram_tensor("wd_t", [FH, E], I32, kind="ExternalInput")
    kv_in = nc.dram_tensor("kvalid", [T], BF16, kind="ExternalInput")
    mask_in = nc.dram_tensor("masks", [128, 2048], BF16, kind="ExternalInput")
    id_in = nc.dram_tensor("ident", [128, 128], F32, kind="ExternalInput")
    y_out = nc.dram_tensor("y_out", [TQ, E], F32, kind="ExternalOutput")

    s_wA = nc.dram_tensor("s_wA", [16, 128, 1024], BF16)
    s_wV = nc.dram_tensor("s_wV", [2, 128, 4096], BF16)
    s_wout = nc.dram_tensor("s_wout", [8, 128, 1024], BF16)
    s_wg = nc.dram_tensor("s_wg", [24, 128, 1024], BF16)
    s_wu = nc.dram_tensor("s_wu", [24, 128, 1024], BF16)
    s_wd = nc.dram_tensor("s_wd", [24, 128, 1024], BF16)
    s_q = nc.dram_tensor("s_q", [4, 128, 4096], BF16,
                         kind="ExternalOutput" if debug_y else "Internal")
    s_kd = nc.dram_tensor("s_kd", [128, 32768], BF16,
                          kind="ExternalOutput" if debug_y else "Internal")
    s_vd = nc.dram_tensor("s_vd", [128, 33280], BF16,
                          kind="ExternalOutput" if debug_y else "Internal")
    s_ofm = nc.dram_tensor("s_ofm", [4, 128, 4096], BF16,
                           kind="ExternalOutput" if debug_y else "Internal")
    s_y = nc.dram_tensor("s_y", [16, 128, 1024], F32,
                         kind="ExternalOutput" if debug_y else "Internal")

    cursor = [SBUF_BASE]

    offsets = {}

    def alloc(name, rows, cols, dt, at=None):
        sz = mybir.dt.size(dt)
        off = cursor[0] if at is None else at
        off = (off + 31) & ~31
        h = nc.alloc_sbuf_tensor_at(name, [rows, cols], dt, offset=off)
        end = off + cols * sz
        assert end <= SBUF_LIMIT, (name, end)
        if at is None:
            cursor[0] = end
        offsets[name] = off
        return h

    ident = alloc("ident", 128, 128, F32)
    kv_sb = alloc("kv_sb", 128, 32, BF16)
    st = alloc("st", 128, 64, F32)
    sqj = [alloc("sqj%d" % i, 128, 1024, BF16) for i in range(2)]
    drec = alloc("drec", 128, 2048, F32)
    dbc = alloc("dbc", 128, 2048, F32)
    PERS_END = cursor[0]

    kbuf = alloc("kbuf", 128, 8 * 4096, BF16)
    vbuf = alloc("vbuf", 128, 32 * VROW, BF16)
    KV_END = cursor[0]
    xblk = [alloc("xblk%d" % i, 128, 1024, F32) for i in range(2)]
    x1tm = [alloc("x1tm%d" % i, 128, 1024, F32) for i in range(2)]
    x1f = [alloc("x1f%d" % i, 128, 8 * 512, BF16) for i in range(2)]
    wA = [alloc("wA%d" % i, 128, 1024, BF16) for i in range(2)]
    wV = [alloc("wV0", 128, 8 * 512, BF16)]
    qstage = [alloc("qstage0", 128, 512, BF16, at=offsets["drec"])]
    P1_END = cursor[0]

    P0_BASE = (SBUF_LIMIT - 12288) & ~31
    assert P1_END <= P0_BASE, (P1_END, P0_BASE)
    inb = [alloc("inb%d" % i, 128, 1024, F32, at=P0_BASE + i * 4096)
           for i in range(2)]
    inbi = [nc.alloc_sbuf_tensor_at("inbi%d" % i, [128, 1024], I32,
                                    offset=P0_BASE + i * 4096) for i in range(2)]
    cbB = [alloc("cb%d" % i, 128, 1024, BF16, at=P0_BASE + 8192 + i * 2048)
           for i in range(2)]

    cursor[0] = KV_END
    maskb = alloc("maskb", 128, 2048, BF16)
    qbuf = [alloc("qbuf0", 128, 8 * 512, BF16)]
    pbuf = [alloc("pbuf%d" % i, 128, 2048, BF16) for i in range(2)]
    ofm = alloc("ofm", 128, 8 * 512, BF16)
    stg = alloc("stg", 128, 8 * 512, BF16)
    asb = [alloc("asb%d" % i, 128, 1024, F32) for i in range(2)]
    lna = alloc("lna", 128, 1024, F32)
    xres = alloc("xres", 128, 1024, F32)
    wo = [alloc("wo%d" % i, 128, 1024, BF16) for i in range(2)]
    P2_END = cursor[0]
    assert P2_END <= SBUF_LIMIT, P2_END

    cursor[0] = PERS_END
    wd_sb = alloc("wd_sb", 128, 24 * 1024, BF16)
    ybuf = [alloc("ybuf%d" % i, 128, 1024, F32) for i in range(8)]
    x2f = alloc("x2f", 128, 8 * 1024, BF16)
    hbuf = alloc("hbuf", 128, 24 * 1024, BF16)
    wgu = [alloc("wgu%d" % i, 128, 1024, BF16) for i in range(4)]
    x2tm = [alloc("x2tm%d" % i, 128, 1024, F32) for i in range(2)]
    fsb = alloc("fsb", 128, 1024, F32)
    lnf = alloc("lnf", 128, 1024, F32)
    outb = alloc("outb", 128, 1024, F32)
    gt = alloc("gt", 128, 512, BF16)
    P3_END = cursor[0]
    assert P3_END <= SBUF_LIMIT, P3_END

    import os
    if os.environ.get("DUMP_OFFSETS"):
        for k in sorted(offsets, key=offsets.get):
            print(k, offsets[k])
    ps = nc.alloc_psum_tensor("ps", [128, 4096], F32)

    def bank(i, r0=0, r1=128, c0=0, c1=512):
        return ps[r0:r1, i * 512 + c0:i * 512 + c1]

    mm, vv, ss = (Sem(n) for n in ["mm", "vv", "ss"])
    dW = DmaQ("dW", 8)
    dS = DmaQ("dS", 6)
    dG = DmaQ("dG", 4)
    P = Prog()

    def dma(q, out_ap, in_ap, sem, waits=(), slow=False):
        tag = sem.next_tag()
        lane, val = tag
        prev = (lane, val - 16)
        P.op(q, lambda e, o=out_ap, i=in_ap, s=slow: e.dma_start(
            o, i, allow_slow_non_contiguous=s).then_inc(lane.h, 16),
            waits=waits)
        return tag

    def barrier():
        tags = ([(s, s.n) for s in [mm, vv, ss]] + dW.all_tags()
                + dS.all_tags() + dG.all_tags())
        for engn in ["sync", "scalar", "vector", "tensor", "gpsimd"]:
            P.op(engn, lambda e: e.nop(), waits=tags)

    ln_slot = [0]
    sq_free = [None, None]
    slot_free = [None] * 8

    class LN:

        def __init__(self, x_ap, out_ap, waits=(), out_waits=()):
            self.x = x_ap
            self.out = out_ap
            self.waits = list(waits)
            self.out_waits = list(out_waits)
            self.slot = ln_slot[0] % 8
            self.par = ln_slot[0] % 2
            ln_slot[0] += 1
            self.c = st[:, self.slot * 8:self.slot * 8 + 8]

        def a(self):
            c = self.c
            P.op("vector", lambda e: e.reduce_sum(c[:, 0:1], self.x, axis=AXX),
                 waits=self.waits + [slot_free[self.slot]])
            P.op("vector", lambda e: e.drain())
            t_nm = P.op("vector", lambda e: e.tensor_scalar(
                out=c[:, 1:2], in0=c[:, 0:1], scalar1=-1.0 / E, scalar2=None,
                op0=ALU.mult), inc=vv, dr=True)
            self.t_sq = P.op("scalar", lambda e, par=self.par: [
                e.activation(out=sqj[par][:, :], in_=self.x, func=AF.Square,
                             bias=c[:, 1:2]),
                e.drain(),
            ], waits=[t_nm, sq_free[self.par]], inc=ss)

        def b(self):
            c = self.c
            t_rs = P.op("vector", lambda e, par=self.par: e.reduce_sum(
                c[:, 2:3], sqj[par][:, :], axis=AXX), waits=[self.t_sq], inc=vv)
            sq_free[self.par] = t_rs
            P.op("vector", lambda e: e.drain())
            P.op("vector", lambda e: e.tensor_scalar(
                out=c[:, 3:4], in0=c[:, 2:3], scalar1=1.0 / E, scalar2=EPS,
                op0=ALU.mult, op1=ALU.add))
            P.op("vector", lambda e: e.drain())
            t_rc = P.op("vector", lambda e: e.reciprocal(c[:, 4:5], c[:, 3:4]),
                        inc=vv, dr=True)
            self.t_r = P.op("scalar", lambda e: e.activation(
                out=c[:, 5:6], in_=c[:, 4:5], func=AF.Sqrt), waits=[t_rc],
                inc=ss, dr=True)
            slot_free[self.slot] = self.t_r

        def cfin(self):
            c = self.c
            P.op("vector", lambda e: e.tensor_tensor(
                out=c[:, 6:7], in0=c[:, 1:2], in1=c[:, 5:6], op=ALU.mult),
                waits=[self.t_r])
            P.op("vector", lambda e: e.drain())
            return P.op("vector", lambda e: e.tensor_scalar(
                out=self.out, in0=self.x, scalar1=c[:, 5:6], scalar2=c[:, 6:7],
                op0=ALU.mult, op1=ALU.add), waits=self.out_waits, inc=vv,
                dr=True)

    def layer_norm(x_ap, out_ap, waits=(), out_waits=()):
        ln = LN(x_ap, out_ap, waits, out_waits)
        ln.a()
        ln.b()
        return ln.cfin()

    pc_jobs = []

    def job_inproj(ec, pc3):
        src = wqkv_in[ec * 128:(ec + 1) * 128, pc3 * 1024:(pc3 + 1) * 1024]
        f0 = pc3 * 1024

        def out_fn(e, buf):
            if f0 < 2048:
                return [e.dma_start(
                    bass.AP(s_wA, (f0 // 128) * 131072 + ec * 128,
                            [[1024, 128], [131072, 8], [1, 128]]),
                    buf[:, 0:1024].rearrange("p (n f) -> p n f", n=8))]
            return [e.dma_start(
                bass.AP(s_wV, ec * 512, [[4096, 128], [524288, 2], [1, 512]]),
                buf[:, 0:1024].rearrange("p (n f) -> p n f", n=2))]
        return (src, False, 1024, out_fn, 1)

    def job_simple(src, dst_ap, is_int):
        def out_fn(e, buf):
            return [e.dma_start(dst_ap, buf[:, 0:1024])]
        return (src, is_int, 1024, out_fn, 1)

    def job_scatter(src, s_dst, ec, hc0):
        def out_fn(e, buf):
            return [e.dma_start(
                bass.AP(s_dst, hc0 * 131072 + ec * 128,
                        [[1024, 128], [131072, 8], [1, 128]]),
                buf[:, 0:1024].rearrange("p (n f) -> p n f", n=8))]
        return (src, True, 1024, out_fn, 1)

    for pc3 in (1, 2, 0):
        for ec in range(8):
            pc_jobs.append(job_inproj(ec, pc3))
    for ec in range(8):
        pc_jobs.append(job_simple(wout_in[ec * 128:(ec + 1) * 128, :],
                                  s_wout[ec, :, :], False))
    P0A = len(pc_jobs)
    pc_tags = []
    for (w, s_dst) in [(wg_in, s_wg), (wu_in, s_wu)]:
        for ec in range(8):
            for pc3 in range(3):
                pc_jobs.append(job_scatter(
                    w[ec * 128:(ec + 1) * 128, pc3 * 1024:(pc3 + 1) * 1024],
                    s_dst, ec, pc3 * 8))
    for hc in range(24):
        pc_jobs.append(job_simple(wd_in[hc * 128:(hc + 1) * 128, :],
                                  s_wd[hc, :, :], True))

    pc_i = [0]
    pc_buf_free = [None, None]

    def precast_step():
        i = pc_i[0]
        if i >= len(pc_jobs):
            return
        pc_i[0] += 1
        src, is_int, ncols, out_fn, nd = pc_jobs[i]
        b = i % 2
        t_ld = dma("sync", (inbi[b] if is_int else inb[b])[:, 0:ncols], src, dW,
                   waits=[pc_buf_free[b]])
        if is_int:
            t_c = P.op("vector", lambda e, b=b, n=ncols: e.tensor_scalar(
                out=cbB[b][:, 0:n], in0=inbi[b][:, 0:n], scalar1=1.0 / 127.0,
                scalar2=None, op0=ALU.mult), waits=[t_ld], inc=vv)
        else:
            t_c = P.op("vector", lambda e, b=b, n=ncols: e.tensor_copy(
                cbB[b][:, 0:n], inb[b][:, 0:n]), waits=[t_ld], inc=vv)
        tag = dS.next_tag()
        lane, val = tag
        P.op("scalar", lambda e, f=out_fn, b=b, ln=lane: [
            i.then_inc(ln.h, 16) for i in f(e, cbB[b])][-1] and None,
            waits=[t_c])
        t_st = tag
        pc_tags.append(tag)
        pc_buf_free[b] = t_st

    for _ in range(P0A):
        precast_step()
    gK = pc_tags[0:8]
    gV = pc_tags[8:16]
    gQ = pc_tags[16:24]
    gO = pc_tags[24:32]
    t_p0a = dS.all_tags()

    t_kv = dma("sync", kv_sb[:, :], bass.AP(kv_in, 0, [[1, 128], [128, 32]]), dW,
               slow=True)
    t_id = dma("sync", ident[:, :], id_in[:, :], dW)

    x_free = {}
    x1tm_free = [None, None]
    x1f_free = [None, None]
    trb_free = [None, None]
    qkb_free = {}
    wA_free = [None, None]
    wV_free = [None]
    qst_free = [None]
    q_spill = {}
    kv_col = [None] * 32

    pcb_per_tb = (len(pc_jobs) - P0A + 7) // 8
    xblk.append(alloc("xblk2", 128, 1024, F32, at=offsets["dbc"]))
    xblk.append(alloc("xblk3", 128, 1024, F32, at=offsets["dbc"] + 4096))
    lns = [None] * 32
    qgens = []

    def interleave_qkv(tb):
        return qkv_units(tb)

    def pump(n):
        while n > 0 and qgens:
            try:
                next(qgens[0])
                n -= 1
            except StopIteration:
                qgens.pop(0)

    def qkv_units(tb):
        par = tb % 2
        has_q = tb >= 4
        t_x1 = (vv, vv.n)
        fcs = list(range(16)) if has_q else list(range(8, 16))

        def do_fc(j, fc):
            wb = j % 2
            t_w = dma("sync", wA[wb][:, :], s_wA[fc, :, :], dW,
                      waits=[wA_free[wb]] + (gQ if fc < 8 else gK))
            pb = j % 2
            t_m = None
            for ecc in range(8):
                t_m = P.op("tensor", lambda e, ecc=ecc, wb=wb, pb=pb, par=par:
                           e.matmul(
                    bank(pb), wA[wb][:, ecc * 128:(ecc + 1) * 128],
                    x1f[par][:, ecc * 512:(ecc + 1) * 512],
                    start=(ecc == 0), stop=(ecc == 7)),
                    waits=([t_w, t_x1, qkb_free.get(pb)] if ecc == 0 else []),
                    inc=(mm if ecc == 7 else None))
            wA_free[wb] = t_m
            x1f_free[par] = t_m
            if fc < 8:
                t_ev = P.op("vector", lambda e, pb=pb: [
                    e.tensor_copy(qstage[0][:, :], bank(pb)), e.drain(),
                ], waits=[t_m, qst_free[0]], inc=vv)
                qkb_free[pb] = t_ev
                t_sp = dma("scalar", s_q[tb - 4, :, fc * 512:(fc + 1) * 512],
                           qstage[0][:, :], dS, waits=[t_ev])
                qst_free[0] = t_sp
                q_spill[tb - 4] = t_sp
            else:
                hp = fc - 8
                t_ev = P.op("vector", lambda e, hp=hp, pb=pb, tb=tb: e.tensor_copy(
                    kbuf[:, hp * 4096 + tb * 512: hp * 4096 + (tb + 1) * 512],
                    bank(pb)), waits=[t_m], inc=vv, dr=True)
                qkb_free[pb] = t_ev

        def do_v(vf, tc):
            wb = 0
            if tc == 0:
                vstate[0] = dma("sync", wV[wb][:, :], s_wV[vf, :, :], dW,
                                waits=[wV_free[wb]] + gV)
            t_wv = vstate[0]
            ch = tb * 4 + tc
            pb = 2 + (tc % 2)
            t_m = None
            for ecc in range(8):
                t_m = P.op("tensor",
                           lambda e, ecc=ecc, tc=tc, pb=pb, par=par, wb=wb:
                           e.matmul(
                    bank(pb),
                    x1f[par][:, ecc * 512 + tc * 128:
                             ecc * 512 + (tc + 1) * 128],
                    wV[wb][:, ecc * 512:(ecc + 1) * 512],
                    start=(ecc == 0), stop=(ecc == 7)),
                    waits=([t_wv, qkb_free.get(pb)] if ecc == 0 else []),
                    inc=(mm if ecc == 7 else None))
            x1f_free[par] = t_m
            base = ch * VROW + vf * 520
            t_ev = P.op("vector", lambda e, base=base, pb=pb: e.tensor_copy(
                bass.AP(vbuf, base, [[32 * VROW, 128], [65, 8], [1, 64]]),
                bass.AP(ps, pb * 512, [[4096, 128], [64, 8], [1, 64]])),
                waits=[t_m], inc=vv, dr=True)
            qkb_free[pb] = t_ev
            if vf == 1:
                kv_col[ch] = P.op("vector", lambda e, ch=ch: [
                    e.tensor_copy(
                        bass.AP(vbuf, ch * VROW + 64,
                                [[32 * VROW, 128], [65, 16]]),
                        kv_sb[:, ch:ch + 1].broadcast_to([128, 16])),
                    e.drain(),
                ], waits=[t_kv, t_ev], inc=vv)
            if vf == 1 and tc == 3:
                wV_free[0] = t_m
            elif tc == 3:
                wV_free[0] = t_m

        vstate = [None]
        vlist = [(vf, tc) for vf in range(2) for tc in range(4)]
        vi = 0
        for j, fc in enumerate(fcs):
            do_fc(j, fc)
            yield
            if j % 2 == 1 and vi < 8:
                do_v(*vlist[vi])
                vi += 1
                yield
        while vi < 8:
            do_v(*vlist[vi])
            vi += 1
            yield

    def interleave_qkv(tb):
        return qkv_units(tb)

    def pump(n):
        while n > 0 and qgens:
            try:
                next(qgens[0])
                n -= 1
            except StopIteration:
                qgens.pop(0)

    def qkv_units(tb):
        par = tb % 2
        has_q = tb >= 4
        t_x1 = (vv, vv.n)
        fcs = list(range(16)) if has_q else list(range(8, 16))
        for j, fc in enumerate(fcs):
            wb = j % 2
            t_w = dma("sync", wA[wb][:, :], s_wA[fc, :, :], dW,
                      waits=[wA_free[wb]] + (gQ if fc < 8 else gK))
            pb = j % 2
            t_m = None
            for ecc in range(8):
                t_m = P.op("tensor", lambda e, ecc=ecc, wb=wb, pb=pb, par=par:
                           e.matmul(
                    bank(pb), wA[wb][:, ecc * 128:(ecc + 1) * 128],
                    x1f[par][:, ecc * 512:(ecc + 1) * 512],
                    start=(ecc == 0), stop=(ecc == 7)),
                    waits=([t_w, t_x1, qkb_free.get(pb)] if ecc == 0 else []),
                    inc=(mm if ecc == 7 else None))
            wA_free[wb] = t_m
            x1f_free[par] = t_m
            if fc < 8:
                t_ev = P.op("vector", lambda e, pb=pb: [
                    e.tensor_copy(qstage[0][:, :], bank(pb)), e.drain(),
                ], waits=[t_m, qst_free[0]], inc=vv)
                qkb_free[pb] = t_ev
                t_sp = dma("scalar", s_q[tb - 4, :, fc * 512:(fc + 1) * 512],
                           qstage[0][:, :], dS, waits=[t_ev])
                qst_free[0] = t_sp
                q_spill[tb - 4] = t_sp
            else:
                hp = fc - 8
                t_ev = P.op("vector", lambda e, hp=hp, pb=pb, tb=tb: e.tensor_copy(
                    kbuf[:, hp * 4096 + tb * 512: hp * 4096 + (tb + 1) * 512],
                    bank(pb)), waits=[t_m], inc=vv, dr=True)
                qkb_free[pb] = t_ev
            yield
        for vf in range(2):
            wb = 0
            t_wv = dma("sync", wV[wb][:, :], s_wV[vf, :, :], dW,
                       waits=[wV_free[wb]] + gV)
            t_m = None
            for tc in range(4):
                ch = tb * 4 + tc
                pb = 2 + (tc % 2)
                for ecc in range(8):
                    t_m = P.op("tensor",
                               lambda e, ecc=ecc, tc=tc, pb=pb, par=par, wb=wb:
                               e.matmul(
                        bank(pb),
                        x1f[par][:, ecc * 512 + tc * 128:
                                 ecc * 512 + (tc + 1) * 128],
                        wV[wb][:, ecc * 512:(ecc + 1) * 512],
                        start=(ecc == 0), stop=(ecc == 7)),
                        waits=([t_wv, qkb_free.get(pb)] if ecc == 0 else []),
                        inc=(mm if ecc == 7 else None))
                x1f_free[par] = t_m
                base = ch * VROW + vf * 520
                t_ev = P.op("vector", lambda e, base=base, pb=pb: e.tensor_copy(
                    bass.AP(vbuf, base, [[32 * VROW, 128], [65, 8], [1, 64]]),
                    bass.AP(ps, pb * 512, [[4096, 128], [64, 8], [1, 64]])),
                    waits=[t_m], inc=vv, dr=True)
                qkb_free[pb] = t_ev
                if vf == 1:
                    kv_col[ch] = P.op("vector", lambda e, ch=ch: [
                        e.tensor_copy(
                            bass.AP(vbuf, ch * VROW + 64,
                                    [[32 * VROW, 128], [65, 16]]),
                            kv_sb[:, ch:ch + 1].broadcast_to([128, 16])),
                        e.drain(),
                    ], waits=[t_kv, t_ev], inc=vv)
                yield
            wV_free[wb] = t_m

    for step in range(34):
        t2 = step - 2
        if 0 <= t2 < 32:
            tb2, tt2 = divmod(t2, 4)
            par2 = tb2 % 2
            bx2 = t2 % 2
            t_ln = lns[t2].cfin()
            x_free[t2 % 4] = t_ln
            trb = t2 % 2
            t_tr = None
            for ecc in range(8):
                t_tr = P.op("tensor", lambda e, ecc=ecc, bx2=bx2, trb=trb:
                            e.transpose(
                    bank(4 + trb * 2 + ecc // 4, c0=(ecc % 4) * 128,
                         c1=(ecc % 4) * 128 + 128),
                    x1tm[bx2][:, ecc * 128:(ecc + 1) * 128], ident[:, :]),
                    waits=([trb_free[trb], t_ln, t_id] if ecc == 0 else []),
                    inc=(mm if ecc == 7 else None))
            x1tm_free[bx2] = t_tr
            t_ev = P.op("vector", lambda e, tt2=tt2, par2=par2, trb=trb: [
                e.tensor_copy(
                    bass.AP(x1f[par2], tt2 * 128,
                            [[4096, 128], [512, 4], [1, 128]]),
                    bank(4 + trb * 2, c1=512).rearrange("p (n c) -> p n c", n=4)),
                e.tensor_copy(
                    bass.AP(x1f[par2], 2048 + tt2 * 128,
                            [[4096, 128], [512, 4], [1, 128]]),
                    bank(5 + trb * 2, c1=512).rearrange("p (n c) -> p n c", n=4)),
                e.drain(),
            ], waits=[t_tr, x1f_free[par2]], inc=vv)
            trb_free[trb] = t_ev
            if tt2 == 3:
                qgens.append(interleave_qkv(tb2))
        t1 = step - 1
        if 0 <= t1 < 32:
            lns[t1].b()
        if step < 32:
            bx = step % 2
            row0 = step * 128
            t_x = dma("gpsimd", xblk[step % 4][:, :], x_in[row0:row0 + 128, :],
                      dG, waits=[x_free.get(step % 4)])
            lns[step] = LN(xblk[step % 4][:, :], x1tm[bx][:, :],
                           waits=[t_x], out_waits=[x1tm_free[bx]])
            lns[step].a()
        pump(7)
        for _ in range(3):
            precast_step()
    pump(10000)
    while pc_i[0] < len(pc_jobs):
        precast_step()

    t_p1_v = (vv, vv.n)
    barrier()
    if debug_y:
        dma("scalar", s_kd[:, :], kbuf[:, :], dS)
        dma("scalar", s_vd[:, :], vbuf[:, :], dS)
    t_mask = dma("sync", maskb[:, :], mask_in[:, :], dW)

    qbuf_free = [None]
    s_free = [None, None]
    pb_free = [None, None]
    oset_free = [None, None]
    drec_free = [None, None]
    dbc_free = [None, None]
    wo_free = [None, None]
    xres_free = None
    asb_free = [None, None]
    stg_free = None
    ofm_free = None
    y_spill = {}
    t_out_last = None

    t_q_next = [None]
    for qb in range(4):
        if t_q_next[0] is None:
            t_q_next[0] = dma("sync", qbuf[0][:, :], s_q[0, :, :], dW,
                              waits=[q_spill[0]])
        t_q = t_q_next[0]
        nch = 20 + 4 * qb
        stage_w = []
        t_pv = None
        for hp in range(8):
            oset = hp % 2
            bA, bB = 4 + 2 * oset, 5 + 2 * oset
            pend = [None]

            def emit_pv(hp=None, ch=None, t_msk=None, po=None, bA=None, bB=None,
                        last=False):
                t = P.op("tensor",
                         lambda e, ch=ch, hp=hp, po=po, bA=bA:
                         e.matmul(
                    bank(bA, r0=0, r1=65),
                    vbuf[:, ch * VROW + 130 * hp: ch * VROW + 130 * hp + 65],
                    pbuf[0][:, po:po + 512],
                    start=(ch == 0), stop=last),
                    waits=[t_msk, kv_col[ch],
                           (oset_free[hp % 2] if ch == 0 else None)],
                    inc=None)
                t = P.op("tensor",
                         lambda e, ch=ch, hp=hp, po=po, bB=bB:
                         e.matmul(
                    bank(bB, r0=0, r1=65),
                    vbuf[:, (ch * VROW + (2 * hp + 1) * 65):
                         (ch * VROW + (2 * hp + 1) * 65 + 65)],
                    pbuf[0][:, po + 512:po + 1024],
                    start=(ch == 0), stop=last), inc=mm)
                return t

            for g in range(nch):
                ch = g
                sset = g % 2
                sb = sset * 1024
                po = sset * 1024
                t_sc = P.op("tensor", lambda e, ch=ch, hp=hp, sb=sb:
                            e.matmul(
                    ps[0:128, sb:sb + 512],
                    kbuf[0:64, hp * 4096 + ch * 128:hp * 4096 + ch * 128 + 128],
                    qbuf[0][0:64, hp * 512:(hp + 1) * 512],
                    start=True, stop=True),
                    waits=[t_q, s_free[sset], t_p1_v],
                    inc=None)
                t_sc = P.op("tensor", lambda e, ch=ch, hp=hp, sb=sb:
                            e.matmul(
                    ps[0:128, sb + 512:sb + 1024],
                    kbuf[64:128, hp * 4096 + ch * 128:hp * 4096 + ch * 128 + 128],
                    qbuf[0][64:128, hp * 512:(hp + 1) * 512],
                    start=True, stop=True, tile_position=(64, 0)), inc=mm)
                if pend[0] is not None:
                    t_pv = pend[0]()
                    pb_free[(g - 1) % 2] = t_pv
                t_exp = P.op("scalar", lambda e, sb=sb, po=po: e.activation(
                    out=pbuf[0][:, po:po + 1024], in_=ps[:, sb:sb + 1024],
                    func=AF.Exp, scale=0.125),
                    waits=[t_sc, pb_free[sset]], inc=ss)
                s_free[sset] = t_exp
                t_msk = t_exp
                pat = ch - (16 + 4 * qb)
                if 0 <= pat < 4:
                    t_msk = P.op("vector", lambda e, po=po, pat=pat: [
                        e.tensor_mul(pbuf[0][:, po:po + 512],
                                     pbuf[0][:, po:po + 512],
                                     maskb[:, pat * 512:(pat + 1) * 512]),
                        e.tensor_mul(pbuf[0][:, po + 512:po + 1024],
                                     pbuf[0][:, po + 512:po + 1024],
                                     maskb[:, pat * 512:(pat + 1) * 512]),
                        e.drain(),
                    ], waits=[t_exp, t_mask], inc=vv)
                pend[0] = (lambda hp=hp, ch=ch, t_msk=t_msk, po=po, bA=bA,
                           bB=bB, last=(ch == nch - 1):
                           emit_pv(hp, ch, t_msk, po, bA, bB, last))
            t_pv = pend[0]()
            pb_free[(nch - 1) % 2] = t_pv
            doA = oset * 1024
            doB = doA + 512
            do2 = oset * 1024
            drA = bass.AP(drec, 64 * 2048 + doA, [[2048, 1], [1, 512]])
            drB = bass.AP(drec, 64 * 2048 + doB, [[2048, 1], [1, 512]])
            t_d = P.op("vector", lambda e, bA=bA, bB=bB, drA=drA, drB=drB: [
                e.tensor_copy(drA, bank(bA, r0=64, r1=65)),
                e.tensor_copy(drB, bank(bB, r0=64, r1=65)),
                e.drain(),
                e.reciprocal(drA, drA),
                e.reciprocal(drB, drB),
                e.drain(),
            ], waits=[t_pv, drec_free[oset]], inc=vv)
            t_b1 = dma("gpsimd", bass.AP(dbc, do2, [[2048, 64], [1, 512]]),
                       bass.AP(drec, 64 * 2048 + doA,
                               [[2048, 1], [0, 64], [1, 512]]), dG,
                       waits=[t_d, dbc_free[oset]])
            t_b2 = dma("gpsimd",
                       bass.AP(dbc, do2 + 512, [[2048, 64], [1, 512]]),
                       bass.AP(drec, 64 * 2048 + doB,
                               [[2048, 1], [0, 64], [1, 512]]),
                       dG, waits=[t_d, dbc_free[oset]])
            drec_free[oset] = t_b2
            if NO_DIV[0]:
                t_nm = P.op("vector", lambda e, hp=hp, bA=bA, bB=bB: [
                    e.tensor_copy(ofm[0:64, hp * 512:(hp + 1) * 512],
                                  bank(bA, r0=0, r1=64)),
                    e.tensor_copy(stg[0:64, hp * 512:(hp + 1) * 512],
                                  bank(bB, r0=0, r1=64)),
                    e.drain(),
                ], waits=[t_pv, stg_free, ofm_free], inc=vv)
            else:
                t_nm = P.op("vector", lambda e, hp=hp, bA=bA, bB=bB, do2=do2: [
                    e.tensor_mul(ofm[0:64, hp * 512:(hp + 1) * 512],
                                 bank(bA, r0=0, r1=64),
                                 bass.AP(dbc, do2, [[2048, 64], [1, 512]])),
                    e.tensor_mul(stg[0:64, hp * 512:(hp + 1) * 512],
                                 bank(bB, r0=0, r1=64),
                                 bass.AP(dbc, do2 + 512, [[2048, 64], [1, 512]])),
                    e.drain(),
                ], waits=[t_b1, t_b2, stg_free, ofm_free], inc=vv)
            dbc_free[oset] = t_nm
            oset_free[oset] = t_nm
            stage_w.append(t_nm)
        qbuf_free[0] = t_pv
        t_sh = dma("gpsimd", ofm[64:128, :], stg[0:64, :], dG, waits=stage_w)

        if debug_y:
            dma("scalar", s_ofm[qb, :, :], ofm[:, :], dS, waits=[t_sh])
        t_ae = None
        for tp in range(2):
            t_mO = None
            for ecc in range(8):
                wb = ecc % 2
                t_w = dma("sync", wo[wb][:, :], s_wout[ecc, :, :], dW,
                          waits=[wo_free[wb]] + gO)
                for sub in range(4):
                    tc = tp * 2 + sub // 2
                    eoh = sub % 2
                    t_mO = P.op("tensor",
                                lambda e, ecc=ecc, wb=wb, tc=tc, eoh=eoh, sub=sub:
                                e.matmul(
                        bank(4 + sub),
                        ofm[:, ecc * 512 + tc * 128: ecc * 512 + (tc + 1) * 128],
                        wo[wb][:, eoh * 512:(eoh + 1) * 512],
                        start=(ecc == 0), stop=(ecc == 7)),
                        waits=([t_w] + ([t_sh, oset_free[0], oset_free[1],
                                         asb_free[0], asb_free[1], t_ae]
                                        if ecc == 0 else [])),
                        inc=(mm if sub == 3 else None), dr=(ecc == 7))
                wo_free[wb] = t_mO
            for sub2 in range(2):
                tc = tp * 2 + sub2
                t_ae = P.op("vector", lambda e, sub2=sub2, tc=tc: [
                    e.tensor_copy(asb[tc % 2][:, 0:512], bank(4 + sub2 * 2)),
                    e.tensor_copy(asb[tc % 2][:, 512:1024], bank(5 + sub2 * 2)),
                ], waits=[t_mO, asb_free[tc % 2]], inc=vv)
                t_l = layer_norm(asb[tc % 2][:, :], lna[:, :], waits=[t_ae])
                asb_free[tc % 2] = t_l
                row0 = 2048 + qb * 512 + tc * 128
                t_xr = dma("gpsimd", xres[:, :], x_in[row0:row0 + 128, :], dG,
                           waits=[xres_free])
                t_y = P.op("vector", lambda e: [
                    e.tensor_add(xres[:, :], xres[:, :], lna[:, :]), e.drain(),
                ], waits=[t_l, t_xr], inc=vv)
                t_sp = dma("scalar", s_y[qb * 4 + tc, :, :], xres[:, :], dS,
                           waits=[t_y])
                xres_free = t_sp
                y_spill[qb * 4 + tc] = t_sp
        oset_free[0] = t_ae
        oset_free[1] = t_ae
        stg_free = t_sh
        ofm_free = t_mO
        if qb + 1 < 4:
            t_q_next[0] = dma("sync", qbuf[0][:, :], s_q[qb + 1, :, :], dW,
                              waits=[q_spill[qb + 1], qbuf_free[0]])

    t_p2_v = (vv, vv.n)
    t_p2_s = (ss, ss.n)
    t_pc_all = dS.all_tags()
    barrier()

    t_wd = None
    for hc in range(24):
        t_wd = dma("sync", wd_sb[:, hc * 1024:(hc + 1) * 1024], s_wd[hc, :, :],
                   dW, waits=[t_pc_all] if hc == 0 else [])

    ybuf_free = [None] * 8
    x2f_free = None
    x2tm_free3 = [None, None]
    wgu_free = [None] * 4
    gu_free = {}
    f_free = {}
    trb3_free = None
    t_out = None

    for sbk in range(2):
        hbuf_w = []
        lns3 = [None] * 8
        th_gate = [None, None]
        gu_gens = []

        def gu_units(th, gate):
            for hc in range(24):
                wbg, wbu = (hc % 2) * 2, (hc % 2) * 2 + 1
                t_wg = dma("sync", wgu[wbg][:, :], s_wg[hc, :, :], dW,
                           waits=[wgu_free[wbg], t_pc_all])
                t_wu = dma("sync", wgu[wbu][:, :], s_wu[hc, :, :], dW,
                           waits=[wgu_free[wbu]])
                gb = ((hc * 2 + th) % 2) * 2
                ub = gb + 1
                t_mg = None
                for ecc in range(8):
                    t_mg = P.op("tensor",
                                lambda e, ecc=ecc, wbg=wbg, th=th, gb=gb:
                                e.matmul(
                        bank(gb), wgu[wbg][:, ecc * 128:(ecc + 1) * 128],
                        x2f[:, ecc * 1024 + th * 512:
                            ecc * 1024 + th * 512 + 512],
                        start=(ecc == 0), stop=(ecc == 7)),
                        waits=([t_wg, gate, gu_free.get(gb)]
                               if ecc == 0 else []),
                        inc=(mm if ecc == 7 else None))
                t_mu = None
                for ecc in range(8):
                    t_mu = P.op("tensor",
                                lambda e, ecc=ecc, wbu=wbu, th=th, ub=ub:
                                e.matmul(
                        bank(ub), wgu[wbu][:, ecc * 128:(ecc + 1) * 128],
                        x2f[:, ecc * 1024 + th * 512:
                            ecc * 1024 + th * 512 + 512],
                        start=(ecc == 0), stop=(ecc == 7)),
                        waits=([t_wu, gu_free.get(ub)] if ecc == 0 else []),
                        inc=(mm if ecc == 7 else None))
                wgu_free[wbg] = t_mg
                wgu_free[wbu] = t_mu
                t_si = P.op("scalar", lambda e, gb=gb: e.activation(
                    out=gt[:, :], in_=bank(gb), func=AF.Silu),
                    waits=[t_mg, hbuf_w[-1] if hbuf_w else None], inc=ss,
                    dr=True)
                t_h = P.op("vector", lambda e, hc=hc, th=th, ub=ub: e.tensor_mul(
                    hbuf[:, hc * 1024 + th * 512: hc * 1024 + th * 512 + 512],
                    gt[:, :], bank(ub)), waits=[t_si, t_mu], inc=vv, dr=True)
                gu_free[gb] = t_h
                gu_free[ub] = t_h
                hbuf_w.append(t_h)
                yield

        def gu_pump(n):
            while n > 0 and gu_gens:
                try:
                    next(gu_gens[0])
                    n -= 1
                except StopIteration:
                    gu_gens.pop(0)

        for step3 in range(10):
            t2 = step3 - 2
            if 0 <= t2 < 8:
                bx = t2 % 2
                t_ln = lns3[t2].cfin()
                t_tr = None
                for ecc in range(8):
                    t_tr = P.op("tensor", lambda e, ecc=ecc, bx=bx: e.transpose(
                        bank(6 + ecc // 4, c0=(ecc % 4) * 128,
                             c1=(ecc % 4) * 128 + 128),
                        x2tm[bx][:, ecc * 128:(ecc + 1) * 128], ident[:, :]),
                        waits=([t_ln, trb3_free] if ecc == 0 else []),
                        inc=(mm if ecc == 7 else None))
                x2tm_free3[bx] = t_tr
                t_ev = P.op("vector", lambda e, tt=t2: [
                    e.tensor_copy(
                        bass.AP(x2f, tt * 128,
                                [[8192, 128], [1024, 4], [1, 128]]),
                        bank(6, c1=512).rearrange("p (n c) -> p n c", n=4)),
                    e.tensor_copy(
                        bass.AP(x2f, 4096 + tt * 128,
                                [[8192, 128], [1024, 4], [1, 128]]),
                        bank(7, c1=512).rearrange("p (n c) -> p n c", n=4)),
                    e.drain(),
                ], waits=[t_tr], inc=vv)
                trb3_free = t_ev
                if t2 == 3:
                    th_gate[0] = (vv, vv.n)
                    gu_gens.append(gu_units(0, th_gate[0]))
                if t2 == 7:
                    th_gate[1] = (vv, vv.n)
                    gu_gens.append(gu_units(1, th_gate[1]))
            t1 = step3 - 1
            if 0 <= t1 < 8:
                lns3[t1].b()
            if step3 < 8:
                gtile = sbk * 8 + step3
                t_yl = dma("gpsimd", ybuf[step3][:, :], s_y[gtile, :, :], dG,
                           waits=[y_spill[gtile], ybuf_free[step3], t_p2_v,
                                  t_p2_s])
                bx = step3 % 2
                lns3[step3] = LN(ybuf[step3][:, :], x2tm[bx][:, :],
                                 waits=[t_yl],
                                 out_waits=[x2tm_free3[bx], x2f_free])
                lns3[step3].a()
            gu_pump(5)
        gu_pump(10000)
        t_hall = hbuf_w[-1]

        t_mf = None
        ln_prev = None
        for tc in range(8):
            for fh in range(2):
                pbx = 4 + fh
                for hc in range(24):
                    t_mf = P.op("tensor",
                                lambda e, hc=hc, tc=tc, fh=fh, pbx=pbx:
                                e.matmul(
                        bank(pbx),
                        hbuf[:, hc * 1024 + tc * 128: hc * 1024 + (tc + 1) * 128],
                        wd_sb[:, hc * 1024 + fh * 512: hc * 1024 + fh * 512 + 512],
                        start=(hc == 0), stop=(hc == 23)),
                        waits=([t_hall, t_wd, f_free.get(pbx)]
                               if hc == 0 else []),
                        inc=(mm if hc == 23 else None))
                f_free[pbx] = t_mf
            t_fe = P.op("vector", lambda e: [
                e.tensor_copy(fsb[:, 0:512], bank(4)),
                e.tensor_copy(fsb[:, 512:1024], bank(5)),
            ], waits=[t_mf, ln_prev], inc=vv)
            f_free[4] = t_fe
            f_free[5] = t_fe
            t_l = layer_norm(fsb[:, :], lnf[:, :], waits=[t_fe])
            ln_prev = t_l
            t_o = P.op("vector", lambda e, tt=tc: [
                e.tensor_add(outb[:, :], ybuf[tt][:, :], lnf[:, :]), e.drain(),
            ], waits=[t_l], inc=vv)
            ybuf_free[tc] = t_o
            gtile = sbk * 8 + tc
            t_out = dma("scalar", y_out[gtile * 128:(gtile + 1) * 128, :],
                        outb[:, :], dS, waits=[t_o])
        x2f_free = t_mf

    P.op("sync", lambda e: e.nop(),
         waits=[t_out] + dW.all_tags() + dG.all_tags() + dS.all_tags())

    all_sems = [mm, vv, ss] + dW.lanes + dS.lanes + dG.lanes
    import contextlib
    with contextlib.ExitStack() as stack:
        for s in all_sems:
            s.h = stack.enter_context(nc.semaphore("sem_" + s.name))
        block = stack.enter_context(nc.Block())

        @block.sync
        def _(e):
            P.emit("sync", e)

        @block.scalar
        def _(e):
            P.emit("scalar", e)

        @block.vector
        def _(e):
            P.emit("vector", e)

        @block.tensor
        def _(e):
            P.emit("tensor", e)

        @block.gpsimd
        def _(e):
            P.emit("gpsimd", e)

    return nc


_NC_CACHE = [None]


def _masks():
    m = np.zeros((128, 2048), dtype=np.float32)
    for pat in range(4):
        for p in range(128):
            c0 = pat * 128 + p
            m[p, pat * 512 + c0: pat * 512 + 512] = 1.0
    return m.astype(ml_dtypes.bfloat16)


def kernel(L0_attn_query, L0_attn_key, L0_attn_value, L0_attn_in_proj_weight,
           L0_attn_out_proj_weight, L0_ffn_w_gate, L0_ffn_w_up, L0_ffn_w_down):
    x = np.ascontiguousarray(np.asarray(L0_attn_query, dtype=np.float32))
    wqkv_t = np.ascontiguousarray(
        np.asarray(L0_attn_in_proj_weight, dtype=np.float32).T)
    wout_t = np.ascontiguousarray(
        np.asarray(L0_attn_out_proj_weight, dtype=np.float32).T)
    wg_t = np.ascontiguousarray(np.asarray(L0_ffn_w_gate, dtype=np.int32).T)
    wu_t = np.ascontiguousarray(np.asarray(L0_ffn_w_up, dtype=np.int32).T)
    wd_t = np.ascontiguousarray(np.asarray(L0_ffn_w_down, dtype=np.int32).T)
    masks = _masks()
    ident = np.eye(128, dtype=np.float32)
    kv1 = np.ones(T, dtype=ml_dtypes.bfloat16)
    kv0 = np.concatenate([np.zeros(TQ), np.ones(TQ)]).astype(ml_dtypes.bfloat16)

    if _NC_CACHE[0] is None:
        _NC_CACHE[0] = build_program()
    nc = _NC_CACHE[0]

    in_maps = []
    for c in range(8):
        b, r = c // 2, c % 2
        if r == 1:
            xl = x[b]
        else:
            xl = np.concatenate(
                [np.zeros((TQ, E), dtype=np.float32), x[b, :TQ]], axis=0)
        in_maps.append({
            "x_local": np.ascontiguousarray(xl),
            "wqkv_t": wqkv_t, "wout_t": wout_t,
            "wg_t": wg_t, "wu_t": wu_t, "wd_t": wd_t,
            "kvalid": kv1 if r == 1 else kv0,
            "masks": masks, "ident": ident,
        })
    res = run_bass_kernel_spmd(nc, in_maps, core_ids=list(range(8)))
    out = np.empty((4, T, E), dtype=np.float32)
    for c in range(8):
        b, r = c // 2, c % 2
        y = res.results[c]["y_out"]
        if r == 1:
            out[b, TQ:] = y
        else:
            out[b, :TQ] = y
    return out




# version 10
# speedup vs baseline: 1.0289x; 1.0289x over previous
import numpy as np
import ml_dtypes

import concourse.bass as bass
import concourse.mybir as mybir
from concourse.bass_utils import run_bass_kernel_spmd

F32 = mybir.dt.float32
BF16 = mybir.dt.bfloat16
I32 = mybir.dt.int32
AF = mybir.ActivationFunctionType
ALU = mybir.AluOpType
AXX = mybir.AxisListType.X
TENSOR_DRAIN = [True]
NO_DIV = [False]
EXP_SPLIT = [False]

T = 4096
TQ = 2048
E = 1024
H = 16
FH = 3072
VROW = H * 65
EPS = 1e-5
SBUF_BASE = 16384 + 2048
SBUF_LIMIT = 229184


class Sem:
    def __init__(self, name):
        self.name = name
        self.n = 0
        self.h = None


class DmaQ:

    def __init__(self, name, nlanes):
        self.lanes = [Sem("%s%d" % (name, i)) for i in range(nlanes)]
        self.i = 0

    def next_tag(self):
        lane = self.lanes[self.i % len(self.lanes)]
        self.i += 1
        lane.n += 16
        return (lane, lane.n)

    def all_tags(self):
        return [(l, l.n) for l in self.lanes]


class Prog:
    def __init__(self):
        self.ops = []

    def op(self, eng, fn, waits=(), inc=None, amt=1, ndma=None, dr=False):
        flat = []
        for w in waits:
            if w is None:
                continue
            if isinstance(w, list):
                flat.extend(x for x in w if x is not None)
            else:
                flat.append(w)
        waits = flat
        if inc is None:
            self.ops.append((eng, fn, waits, None))
            return None
        if ndma is not None:
            inc.n += 16 * ndma
            self.ops.append((eng, fn, waits, (inc, 16, True, dr)))
        else:
            inc.n += amt
            self.ops.append((eng, fn, waits, (inc, amt, False, dr)))
        return (inc, inc.n)

    def emit(self, eng_name, eng):
        last_wait = {}
        for e, fn, waits, inc in self.ops:
            if e != eng_name:
                continue
            for (sem, val) in waits:
                if last_wait.get(sem.name, -1) >= val:
                    continue
                eng.wait_ge(sem.h, val)
                last_wait[sem.name] = val
            inst = fn(eng)
            if inc is not None:
                sem, amt, each, dr = inc
                insts = inst if isinstance(inst, list) else [inst]
                if each:
                    for i in insts:
                        i.then_inc(sem.h, amt)
                elif dr:
                    eng.drain().then_inc(sem.h, amt)
                else:
                    insts[-1].then_inc(sem.h, amt)


def build_program(debug_y=False, paranoid=0):
    nc = bass.Bass("TRN2", target_bir_lowering=False)

    x_in = nc.dram_tensor("x_local", [T, E], F32, kind="ExternalInput")
    wqkv_in = nc.dram_tensor("wqkv_t", [E, 3 * E], F32, kind="ExternalInput")
    wout_in = nc.dram_tensor("wout_t", [E, E], F32, kind="ExternalInput")
    wg_in = nc.dram_tensor("wg_t", [E, FH], I32, kind="ExternalInput")
    wu_in = nc.dram_tensor("wu_t", [E, FH], I32, kind="ExternalInput")
    wd_in = nc.dram_tensor("wd_t", [FH, E], I32, kind="ExternalInput")
    kv_in = nc.dram_tensor("kvalid", [T], BF16, kind="ExternalInput")
    mask_in = nc.dram_tensor("masks", [128, 2048], BF16, kind="ExternalInput")
    id_in = nc.dram_tensor("ident", [128, 128], F32, kind="ExternalInput")
    y_out = nc.dram_tensor("y_out", [TQ, E], F32, kind="ExternalOutput")

    s_wA = nc.dram_tensor("s_wA", [16, 128, 1024], BF16)
    s_wV = nc.dram_tensor("s_wV", [2, 128, 4096], BF16)
    s_wout = nc.dram_tensor("s_wout", [8, 128, 1024], BF16)
    s_wg = nc.dram_tensor("s_wg", [24, 128, 1024], BF16)
    s_wu = nc.dram_tensor("s_wu", [24, 128, 1024], BF16)
    s_wd = nc.dram_tensor("s_wd", [24, 128, 1024], BF16)
    s_q = nc.dram_tensor("s_q", [4, 128, 4096], BF16,
                         kind="ExternalOutput" if debug_y else "Internal")
    s_kd = nc.dram_tensor("s_kd", [128, 32768], BF16,
                          kind="ExternalOutput" if debug_y else "Internal")
    s_vd = nc.dram_tensor("s_vd", [128, 33280], BF16,
                          kind="ExternalOutput" if debug_y else "Internal")
    s_ofm = nc.dram_tensor("s_ofm", [4, 128, 4096], BF16,
                           kind="ExternalOutput" if debug_y else "Internal")
    s_y = nc.dram_tensor("s_y", [16, 128, 1024], F32,
                         kind="ExternalOutput" if debug_y else "Internal")

    cursor = [SBUF_BASE]

    offsets = {}

    def alloc(name, rows, cols, dt, at=None):
        sz = mybir.dt.size(dt)
        off = cursor[0] if at is None else at
        off = (off + 31) & ~31
        h = nc.alloc_sbuf_tensor_at(name, [rows, cols], dt, offset=off)
        end = off + cols * sz
        assert end <= SBUF_LIMIT, (name, end)
        if at is None:
            cursor[0] = end
        offsets[name] = off
        return h

    ident = alloc("ident", 128, 128, F32)
    kv_sb = alloc("kv_sb", 128, 32, BF16)
    st = alloc("st", 128, 64, F32)
    sqj = [alloc("sqj%d" % i, 128, 1024, BF16) for i in range(2)]
    drec = alloc("drec", 128, 2048, F32)
    dbc = alloc("dbc", 128, 2048, F32)
    PERS_END = cursor[0]

    kbuf = alloc("kbuf", 128, 8 * 4096, BF16)
    vbuf = alloc("vbuf", 128, 32 * VROW, BF16)
    KV_END = cursor[0]
    xblk = [alloc("xblk%d" % i, 128, 1024, F32) for i in range(2)]
    x1tm = [alloc("x1tm%d" % i, 128, 1024, F32) for i in range(2)]
    x1f = [alloc("x1f%d" % i, 128, 8 * 512, BF16) for i in range(2)]
    wA = [alloc("wA%d" % i, 128, 1024, BF16) for i in range(2)]
    wV = [alloc("wV0", 128, 8 * 512, BF16)]
    qstage = [alloc("qstage0", 128, 512, BF16, at=offsets["drec"])]
    P1_END = cursor[0]

    P0_BASE = (SBUF_LIMIT - 12288) & ~31
    assert P1_END <= P0_BASE, (P1_END, P0_BASE)
    inb = [alloc("inb%d" % i, 128, 1024, F32, at=P0_BASE + i * 4096)
           for i in range(2)]
    inbi = [nc.alloc_sbuf_tensor_at("inbi%d" % i, [128, 1024], I32,
                                    offset=P0_BASE + i * 4096) for i in range(2)]
    cbB = [alloc("cb%d" % i, 128, 1024, BF16, at=P0_BASE + 8192 + i * 2048)
           for i in range(2)]

    cursor[0] = KV_END
    maskb = alloc("maskb", 128, 2048, BF16)
    qbuf = [alloc("qbuf0", 128, 8 * 512, BF16)]
    pbuf = [alloc("pbuf%d" % i, 128, 2048, BF16) for i in range(2)]
    ofm = alloc("ofm", 128, 8 * 512, BF16)
    stg = alloc("stg", 128, 8 * 512, BF16)
    asb = [alloc("asb%d" % i, 128, 1024, F32) for i in range(2)]
    lna = alloc("lna", 128, 1024, F32)
    xres = alloc("xres", 128, 1024, F32)
    wo = [alloc("wo%d" % i, 128, 1024, BF16) for i in range(2)]
    P2_END = cursor[0]
    assert P2_END <= SBUF_LIMIT, P2_END

    cursor[0] = PERS_END
    wd_sb = alloc("wd_sb", 128, 24 * 1024, BF16)
    ybuf = [alloc("ybuf%d" % i, 128, 1024, F32) for i in range(8)]
    x2f = alloc("x2f", 128, 8 * 1024, BF16)
    hbuf = alloc("hbuf", 128, 24 * 1024, BF16)
    wgu = [alloc("wgu%d" % i, 128, 1024, BF16) for i in range(4)]
    x2tm = [alloc("x2tm%d" % i, 128, 1024, F32) for i in range(2)]
    fsb = alloc("fsb", 128, 1024, F32)
    lnf = alloc("lnf", 128, 1024, F32)
    outb = alloc("outb", 128, 1024, F32)
    gt = alloc("gt", 128, 512, BF16)
    P3_END = cursor[0]
    assert P3_END <= SBUF_LIMIT, P3_END

    import os
    if os.environ.get("DUMP_OFFSETS"):
        for k in sorted(offsets, key=offsets.get):
            print(k, offsets[k])
    ps = nc.alloc_psum_tensor("ps", [128, 4096], F32)

    def bank(i, r0=0, r1=128, c0=0, c1=512):
        return ps[r0:r1, i * 512 + c0:i * 512 + c1]

    mm, vv, ss = (Sem(n) for n in ["mm", "vv", "ss"])
    dW = DmaQ("dW", 8)
    dS = DmaQ("dS", 6)
    dG = DmaQ("dG", 4)
    P = Prog()

    def dma(q, out_ap, in_ap, sem, waits=(), slow=False):
        tag = sem.next_tag()
        lane, val = tag
        prev = (lane, val - 16)
        P.op(q, lambda e, o=out_ap, i=in_ap, s=slow: e.dma_start(
            o, i, allow_slow_non_contiguous=s).then_inc(lane.h, 16),
            waits=waits)
        return tag

    def barrier():
        tags = ([(s, s.n) for s in [mm, vv, ss]] + dW.all_tags()
                + dS.all_tags() + dG.all_tags())
        for engn in ["sync", "scalar", "vector", "tensor", "gpsimd"]:
            P.op(engn, lambda e: e.nop(), waits=tags)

    ln_slot = [0]
    sq_free = [None, None]
    slot_free = [None] * 8

    class LN:

        def __init__(self, x_ap, out_ap, waits=(), out_waits=()):
            self.x = x_ap
            self.out = out_ap
            self.waits = list(waits)
            self.out_waits = list(out_waits)
            self.slot = ln_slot[0] % 8
            self.par = ln_slot[0] % 2
            ln_slot[0] += 1
            self.c = st[:, self.slot * 8:self.slot * 8 + 8]

        def a(self):
            c = self.c
            P.op("vector", lambda e: e.reduce_sum(c[:, 0:1], self.x, axis=AXX),
                 waits=self.waits + [slot_free[self.slot]])
            P.op("vector", lambda e: e.drain())
            t_nm = P.op("vector", lambda e: e.tensor_scalar(
                out=c[:, 1:2], in0=c[:, 0:1], scalar1=-1.0 / E, scalar2=None,
                op0=ALU.mult), inc=vv, dr=True)
            self.t_sq = P.op("scalar", lambda e, par=self.par: [
                e.activation(out=sqj[par][:, :], in_=self.x, func=AF.Square,
                             bias=c[:, 1:2]),
                e.drain(),
            ], waits=[t_nm, sq_free[self.par]], inc=ss)

        def b(self):
            c = self.c
            t_rs = P.op("vector", lambda e, par=self.par: e.reduce_sum(
                c[:, 2:3], sqj[par][:, :], axis=AXX), waits=[self.t_sq], inc=vv)
            sq_free[self.par] = t_rs
            P.op("vector", lambda e: e.drain())
            P.op("vector", lambda e: e.tensor_scalar(
                out=c[:, 3:4], in0=c[:, 2:3], scalar1=1.0 / E, scalar2=EPS,
                op0=ALU.mult, op1=ALU.add))
            P.op("vector", lambda e: e.drain())
            t_rc = P.op("vector", lambda e: e.reciprocal(c[:, 4:5], c[:, 3:4]),
                        inc=vv, dr=True)
            self.t_r = P.op("scalar", lambda e: e.activation(
                out=c[:, 5:6], in_=c[:, 4:5], func=AF.Sqrt), waits=[t_rc],
                inc=ss, dr=True)
            slot_free[self.slot] = self.t_r

        def cfin(self):
            c = self.c
            P.op("vector", lambda e: e.tensor_tensor(
                out=c[:, 6:7], in0=c[:, 1:2], in1=c[:, 5:6], op=ALU.mult),
                waits=[self.t_r])
            P.op("vector", lambda e: e.drain())
            return P.op("vector", lambda e: e.tensor_scalar(
                out=self.out, in0=self.x, scalar1=c[:, 5:6], scalar2=c[:, 6:7],
                op0=ALU.mult, op1=ALU.add), waits=self.out_waits, inc=vv,
                dr=True)

    def layer_norm(x_ap, out_ap, waits=(), out_waits=()):
        ln = LN(x_ap, out_ap, waits, out_waits)
        ln.a()
        ln.b()
        return ln.cfin()

    pc_jobs = []

    def job_inproj(ec, pc3):
        src = wqkv_in[ec * 128:(ec + 1) * 128, pc3 * 1024:(pc3 + 1) * 1024]
        f0 = pc3 * 1024

        def out_fn(e, buf):
            if f0 < 2048:
                return [e.dma_start(
                    bass.AP(s_wA, (f0 // 128) * 131072 + ec * 128,
                            [[1024, 128], [131072, 8], [1, 128]]),
                    buf[:, 0:1024].rearrange("p (n f) -> p n f", n=8))]
            return [e.dma_start(
                bass.AP(s_wV, ec * 512, [[4096, 128], [524288, 2], [1, 512]]),
                buf[:, 0:1024].rearrange("p (n f) -> p n f", n=2))]
        return (src, False, 1024, out_fn, 1)

    def job_simple(src, dst_ap, is_int):
        def out_fn(e, buf):
            return [e.dma_start(dst_ap, buf[:, 0:1024])]
        return (src, is_int, 1024, out_fn, 1)

    def job_scatter(src, s_dst, ec, hc0):
        def out_fn(e, buf):
            return [e.dma_start(
                bass.AP(s_dst, hc0 * 131072 + ec * 128,
                        [[1024, 128], [131072, 8], [1, 128]]),
                buf[:, 0:1024].rearrange("p (n f) -> p n f", n=8))]
        return (src, True, 1024, out_fn, 1)

    for pc3 in (1, 2, 0):
        for ec in range(8):
            pc_jobs.append(job_inproj(ec, pc3))
    for ec in range(8):
        pc_jobs.append(job_simple(wout_in[ec * 128:(ec + 1) * 128, :],
                                  s_wout[ec, :, :], False))
    P0A = len(pc_jobs)
    pc_tags = []
    for (w, s_dst) in [(wg_in, s_wg), (wu_in, s_wu)]:
        for ec in range(8):
            for pc3 in range(3):
                pc_jobs.append(job_scatter(
                    w[ec * 128:(ec + 1) * 128, pc3 * 1024:(pc3 + 1) * 1024],
                    s_dst, ec, pc3 * 8))
    for hc in range(24):
        pc_jobs.append(job_simple(wd_in[hc * 128:(hc + 1) * 128, :],
                                  s_wd[hc, :, :], True))

    pc_i = [0]
    pc_buf_free = [None, None]

    def precast_step():
        i = pc_i[0]
        if i >= len(pc_jobs):
            return
        pc_i[0] += 1
        src, is_int, ncols, out_fn, nd = pc_jobs[i]
        b = i % 2
        t_ld = dma("sync", (inbi[b] if is_int else inb[b])[:, 0:ncols], src, dW,
                   waits=[pc_buf_free[b]])
        if is_int:
            t_c = P.op("vector", lambda e, b=b, n=ncols: e.tensor_scalar(
                out=cbB[b][:, 0:n], in0=inbi[b][:, 0:n], scalar1=1.0 / 127.0,
                scalar2=None, op0=ALU.mult), waits=[t_ld], inc=vv)
        else:
            t_c = P.op("vector", lambda e, b=b, n=ncols: e.tensor_copy(
                cbB[b][:, 0:n], inb[b][:, 0:n]), waits=[t_ld], inc=vv)
        tag = dS.next_tag()
        lane, val = tag
        P.op("scalar", lambda e, f=out_fn, b=b, ln=lane: [
            i.then_inc(ln.h, 16) for i in f(e, cbB[b])][-1] and None,
            waits=[t_c])
        t_st = tag
        pc_tags.append(tag)
        pc_buf_free[b] = t_st

    for _ in range(P0A):
        precast_step()
    gK = pc_tags[0:8]
    gV = pc_tags[8:16]
    gQ = pc_tags[16:24]
    gO = pc_tags[24:32]
    t_p0a = dS.all_tags()

    t_kv = dma("sync", kv_sb[:, :], bass.AP(kv_in, 0, [[1, 128], [128, 32]]), dW,
               slow=True)
    t_id = dma("sync", ident[:, :], id_in[:, :], dW)

    x_free = {}
    x1tm_free = [None, None]
    x1f_free = [None, None]
    trb_free = [None, None]
    qkb_free = {}
    wA_free = [None, None]
    wV_free = [None]
    qst_free = [None]
    q_spill = {}
    kv_col = [None] * 32

    pcb_per_tb = (len(pc_jobs) - P0A + 7) // 8
    xblk.append(alloc("xblk2", 128, 1024, F32, at=offsets["dbc"]))
    xblk.append(alloc("xblk3", 128, 1024, F32, at=offsets["dbc"] + 4096))
    lns = [None] * 32
    qgens = []

    def interleave_qkv(tb):
        return qkv_units(tb)

    def pump(n):
        while n > 0 and qgens:
            try:
                next(qgens[0])
                n -= 1
            except StopIteration:
                qgens.pop(0)

    def qkv_units(tb):
        par = tb % 2
        has_q = tb >= 4
        t_x1 = (vv, vv.n)
        fcs = list(range(16)) if has_q else list(range(8, 16))

        def do_fc(j, fc):
            wb = j % 2
            t_w = dma("sync", wA[wb][:, :], s_wA[fc, :, :], dW,
                      waits=[wA_free[wb]] + (gQ if fc < 8 else gK))
            pb = j % 2
            t_m = None
            for ecc in range(8):
                t_m = P.op("tensor", lambda e, ecc=ecc, wb=wb, pb=pb, par=par:
                           e.matmul(
                    bank(pb), wA[wb][:, ecc * 128:(ecc + 1) * 128],
                    x1f[par][:, ecc * 512:(ecc + 1) * 512],
                    start=(ecc == 0), stop=(ecc == 7)),
                    waits=([t_w, t_x1, qkb_free.get(pb)] if ecc == 0 else []),
                    inc=(mm if ecc == 7 else None))
            wA_free[wb] = t_m
            x1f_free[par] = t_m
            if fc < 8:
                t_ev = P.op("vector", lambda e, pb=pb: [
                    e.tensor_copy(qstage[0][:, :], bank(pb)), e.drain(),
                ], waits=[t_m, qst_free[0]], inc=vv)
                qkb_free[pb] = t_ev
                t_sp = dma("scalar", s_q[tb - 4, :, fc * 512:(fc + 1) * 512],
                           qstage[0][:, :], dS, waits=[t_ev])
                qst_free[0] = t_sp
                q_spill[tb - 4] = t_sp
            else:
                hp = fc - 8
                t_ev = P.op("vector", lambda e, hp=hp, pb=pb, tb=tb: e.tensor_copy(
                    kbuf[:, hp * 4096 + tb * 512: hp * 4096 + (tb + 1) * 512],
                    bank(pb)), waits=[t_m], inc=vv, dr=True)
                qkb_free[pb] = t_ev

        def do_v(vf, tc):
            wb = 0
            if tc == 0:
                vstate[0] = dma("sync", wV[wb][:, :], s_wV[vf, :, :], dW,
                                waits=[wV_free[wb]] + gV)
            t_wv = vstate[0]
            ch = tb * 4 + tc
            pb = 2 + (tc % 2)
            t_m = None
            for ecc in range(8):
                t_m = P.op("tensor",
                           lambda e, ecc=ecc, tc=tc, pb=pb, par=par, wb=wb:
                           e.matmul(
                    bank(pb),
                    x1f[par][:, ecc * 512 + tc * 128:
                             ecc * 512 + (tc + 1) * 128],
                    wV[wb][:, ecc * 512:(ecc + 1) * 512],
                    start=(ecc == 0), stop=(ecc == 7)),
                    waits=([t_wv, qkb_free.get(pb)] if ecc == 0 else []),
                    inc=(mm if ecc == 7 else None))
            x1f_free[par] = t_m
            base = ch * VROW + vf * 520
            t_ev = P.op("vector", lambda e, base=base, pb=pb: e.tensor_copy(
                bass.AP(vbuf, base, [[32 * VROW, 128], [65, 8], [1, 64]]),
                bass.AP(ps, pb * 512, [[4096, 128], [64, 8], [1, 64]])),
                waits=[t_m], inc=vv, dr=True)
            qkb_free[pb] = t_ev
            if vf == 1:
                kv_col[ch] = P.op("vector", lambda e, ch=ch: [
                    e.tensor_copy(
                        bass.AP(vbuf, ch * VROW + 64,
                                [[32 * VROW, 128], [65, 16]]),
                        kv_sb[:, ch:ch + 1].broadcast_to([128, 16])),
                    e.drain(),
                ], waits=[t_kv, t_ev], inc=vv)
            if vf == 1 and tc == 3:
                wV_free[0] = t_m
            elif tc == 3:
                wV_free[0] = t_m

        vstate = [None]
        vlist = [(vf, tc) for vf in range(2) for tc in range(4)]
        vi = 0
        for j, fc in enumerate(fcs):
            do_fc(j, fc)
            yield
            if j % 2 == 1 and vi < 8:
                do_v(*vlist[vi])
                vi += 1
                yield
        while vi < 8:
            do_v(*vlist[vi])
            vi += 1
            yield

    def interleave_qkv(tb):
        return qkv_units(tb)

    def pump(n):
        while n > 0 and qgens:
            try:
                next(qgens[0])
                n -= 1
            except StopIteration:
                qgens.pop(0)

    def qkv_units(tb):
        par = tb % 2
        has_q = tb >= 4
        t_x1 = (vv, vv.n)
        fcs = list(range(16)) if has_q else list(range(8, 16))
        for j, fc in enumerate(fcs):
            wb = j % 2
            t_w = dma("sync", wA[wb][:, :], s_wA[fc, :, :], dW,
                      waits=[wA_free[wb]] + (gQ if fc < 8 else gK))
            pb = j % 2
            t_m = None
            for ecc in range(8):
                t_m = P.op("tensor", lambda e, ecc=ecc, wb=wb, pb=pb, par=par:
                           e.matmul(
                    bank(pb), wA[wb][:, ecc * 128:(ecc + 1) * 128],
                    x1f[par][:, ecc * 512:(ecc + 1) * 512],
                    start=(ecc == 0), stop=(ecc == 7)),
                    waits=([t_w, t_x1, qkb_free.get(pb)] if ecc == 0 else []),
                    inc=(mm if ecc == 7 else None))
            wA_free[wb] = t_m
            x1f_free[par] = t_m
            if fc < 8:
                t_ev = P.op("vector", lambda e, pb=pb: [
                    e.tensor_copy(qstage[0][:, :], bank(pb)), e.drain(),
                ], waits=[t_m, qst_free[0]], inc=vv)
                qkb_free[pb] = t_ev
                t_sp = dma("scalar", s_q[tb - 4, :, fc * 512:(fc + 1) * 512],
                           qstage[0][:, :], dS, waits=[t_ev])
                qst_free[0] = t_sp
                q_spill[tb - 4] = t_sp
            else:
                hp = fc - 8
                t_ev = P.op("vector", lambda e, hp=hp, pb=pb, tb=tb: e.tensor_copy(
                    kbuf[:, hp * 4096 + tb * 512: hp * 4096 + (tb + 1) * 512],
                    bank(pb)), waits=[t_m], inc=vv, dr=True)
                qkb_free[pb] = t_ev
            yield
        for vf in range(2):
            wb = 0
            t_wv = dma("sync", wV[wb][:, :], s_wV[vf, :, :], dW,
                       waits=[wV_free[wb]] + gV)
            t_m = None
            for tc in range(4):
                ch = tb * 4 + tc
                pb = 2 + (tc % 2)
                for ecc in range(8):
                    t_m = P.op("tensor",
                               lambda e, ecc=ecc, tc=tc, pb=pb, par=par, wb=wb:
                               e.matmul(
                        bank(pb),
                        x1f[par][:, ecc * 512 + tc * 128:
                                 ecc * 512 + (tc + 1) * 128],
                        wV[wb][:, ecc * 512:(ecc + 1) * 512],
                        start=(ecc == 0), stop=(ecc == 7)),
                        waits=([t_wv, qkb_free.get(pb)] if ecc == 0 else []),
                        inc=(mm if ecc == 7 else None))
                x1f_free[par] = t_m
                base = ch * VROW + vf * 520
                t_ev = P.op("vector", lambda e, base=base, pb=pb: e.tensor_copy(
                    bass.AP(vbuf, base, [[32 * VROW, 128], [65, 8], [1, 64]]),
                    bass.AP(ps, pb * 512, [[4096, 128], [64, 8], [1, 64]])),
                    waits=[t_m], inc=vv, dr=True)
                qkb_free[pb] = t_ev
                if vf == 1:
                    kv_col[ch] = P.op("vector", lambda e, ch=ch: [
                        e.tensor_copy(
                            bass.AP(vbuf, ch * VROW + 64,
                                    [[32 * VROW, 128], [65, 16]]),
                            kv_sb[:, ch:ch + 1].broadcast_to([128, 16])),
                        e.drain(),
                    ], waits=[t_kv, t_ev], inc=vv)
                yield
            wV_free[wb] = t_m

    for step in range(34):
        t2 = step - 2
        if 0 <= t2 < 32:
            tb2, tt2 = divmod(t2, 4)
            par2 = tb2 % 2
            bx2 = t2 % 2
            t_ln = lns[t2].cfin()
            x_free[t2 % 4] = t_ln
            trb = t2 % 2
            t_tr = None
            for ecc in range(8):
                t_tr = P.op("tensor", lambda e, ecc=ecc, bx2=bx2, trb=trb:
                            e.transpose(
                    bank(4 + trb * 2 + ecc // 4, c0=(ecc % 4) * 128,
                         c1=(ecc % 4) * 128 + 128),
                    x1tm[bx2][:, ecc * 128:(ecc + 1) * 128], ident[:, :]),
                    waits=([trb_free[trb], t_ln, t_id] if ecc == 0 else []),
                    inc=(mm if ecc == 7 else None))
            x1tm_free[bx2] = t_tr
            t_ev = P.op("vector", lambda e, tt2=tt2, par2=par2, trb=trb: [
                e.tensor_copy(
                    bass.AP(x1f[par2], tt2 * 128,
                            [[4096, 128], [512, 4], [1, 128]]),
                    bank(4 + trb * 2, c1=512).rearrange("p (n c) -> p n c", n=4)),
                e.tensor_copy(
                    bass.AP(x1f[par2], 2048 + tt2 * 128,
                            [[4096, 128], [512, 4], [1, 128]]),
                    bank(5 + trb * 2, c1=512).rearrange("p (n c) -> p n c", n=4)),
                e.drain(),
            ], waits=[t_tr, x1f_free[par2]], inc=vv)
            trb_free[trb] = t_ev
            if tt2 == 3:
                qgens.append(interleave_qkv(tb2))
        t1 = step - 1
        if 0 <= t1 < 32:
            lns[t1].b()
        if step < 32:
            bx = step % 2
            row0 = step * 128
            t_x = dma("gpsimd", xblk[step % 4][:, :], x_in[row0:row0 + 128, :],
                      dG, waits=[x_free.get(step % 4)])
            lns[step] = LN(xblk[step % 4][:, :], x1tm[bx][:, :],
                           waits=[t_x], out_waits=[x1tm_free[bx]])
            lns[step].a()
        pump(7)
        for _ in range(3):
            precast_step()
    pump(10000)
    while pc_i[0] < len(pc_jobs):
        precast_step()

    t_p1_v = (vv, vv.n)
    barrier()
    if debug_y:
        dma("scalar", s_kd[:, :], kbuf[:, :], dS)
        dma("scalar", s_vd[:, :], vbuf[:, :], dS)
    t_mask = dma("sync", maskb[:, :], mask_in[:, :], dW)

    qbuf_free = [None]
    s_free = [None, None]
    pb_free = [None, None]
    oset_free = [None, None]
    drec_free = [None, None]
    dbc_free = [None, None]
    wo_free = [None, None]
    xres_free = None
    asb_free = [None, None]
    stg_free = None
    ofm_free = None
    y_spill = {}
    t_out_last = None

    t_q_next = [None]
    for qb in range(4):
        if t_q_next[0] is None:
            t_q_next[0] = dma("sync", qbuf[0][:, :], s_q[0, :, :], dW,
                              waits=[q_spill[0]])
        t_q = t_q_next[0]
        nch = 20 + 4 * qb
        stage_w = []
        t_pv = None
        for hp in range(8):
            oset = hp % 2
            bA, bB = 4 + 2 * oset, 5 + 2 * oset
            pend = [None]

            def emit_pv(hp=None, ch=None, t_msk=None, po=None, bA=None, bB=None,
                        last=False):
                t = P.op("tensor",
                         lambda e, ch=ch, hp=hp, po=po, bA=bA:
                         e.matmul(
                    bank(bA, r0=0, r1=65),
                    vbuf[:, ch * VROW + 130 * hp: ch * VROW + 130 * hp + 65],
                    pbuf[0][:, po:po + 512],
                    start=(ch == 0), stop=last),
                    waits=[t_msk, kv_col[ch],
                           (oset_free[hp % 2] if ch == 0 else None)],
                    inc=None)
                t = P.op("tensor",
                         lambda e, ch=ch, hp=hp, po=po, bB=bB:
                         e.matmul(
                    bank(bB, r0=0, r1=65),
                    vbuf[:, (ch * VROW + (2 * hp + 1) * 65):
                         (ch * VROW + (2 * hp + 1) * 65 + 65)],
                    pbuf[0][:, po + 512:po + 1024],
                    start=(ch == 0), stop=last), inc=mm)
                return t

            for g in range(nch):
                ch = g
                sset = g % 2
                sb = sset * 1024
                po = sset * 1024
                t_sc = P.op("tensor", lambda e, ch=ch, hp=hp, sb=sb:
                            e.matmul(
                    ps[0:128, sb:sb + 512],
                    kbuf[0:64, hp * 4096 + ch * 128:hp * 4096 + ch * 128 + 128],
                    qbuf[0][0:64, hp * 512:(hp + 1) * 512],
                    start=True, stop=True),
                    waits=[t_q, s_free[sset], t_p1_v],
                    inc=None)
                t_sc = P.op("tensor", lambda e, ch=ch, hp=hp, sb=sb:
                            e.matmul(
                    ps[0:128, sb + 512:sb + 1024],
                    kbuf[64:128, hp * 4096 + ch * 128:hp * 4096 + ch * 128 + 128],
                    qbuf[0][64:128, hp * 512:(hp + 1) * 512],
                    start=True, stop=True, tile_position=(64, 0)), inc=mm)
                if pend[0] is not None:
                    t_pv = pend[0]()
                    pb_free[(g - 1) % 2] = t_pv
                t_exp = P.op("scalar", lambda e, sb=sb, po=po: e.activation(
                    out=pbuf[0][:, po:po + 1024], in_=ps[:, sb:sb + 1024],
                    func=AF.Exp, scale=0.125),
                    waits=[t_sc, pb_free[sset]], inc=ss)
                s_free[sset] = t_exp
                t_msk = t_exp
                pat = ch - (16 + 4 * qb)
                if 0 <= pat < 4:
                    t_msk = P.op("vector", lambda e, po=po, pat=pat: [
                        e.tensor_mul(pbuf[0][:, po:po + 512],
                                     pbuf[0][:, po:po + 512],
                                     maskb[:, pat * 512:(pat + 1) * 512]),
                        e.tensor_mul(pbuf[0][:, po + 512:po + 1024],
                                     pbuf[0][:, po + 512:po + 1024],
                                     maskb[:, pat * 512:(pat + 1) * 512]),
                        e.drain(),
                    ], waits=[t_exp, t_mask], inc=vv)
                pend[0] = (lambda hp=hp, ch=ch, t_msk=t_msk, po=po, bA=bA,
                           bB=bB, last=(ch == nch - 1):
                           emit_pv(hp, ch, t_msk, po, bA, bB, last))
            t_pv = pend[0]()
            pb_free[(nch - 1) % 2] = t_pv
            doA = oset * 1024
            doB = doA + 512
            do2 = oset * 1024
            drA = bass.AP(drec, 64 * 2048 + doA, [[2048, 1], [1, 512]])
            drB = bass.AP(drec, 64 * 2048 + doB, [[2048, 1], [1, 512]])
            t_d = P.op("vector", lambda e, bA=bA, bB=bB, drA=drA, drB=drB: [
                e.tensor_copy(drA, bank(bA, r0=64, r1=65)),
                e.tensor_copy(drB, bank(bB, r0=64, r1=65)),
                e.drain(),
                e.reciprocal(drA, drA),
                e.reciprocal(drB, drB),
                e.drain(),
            ], waits=[t_pv, drec_free[oset]], inc=vv)
            t_b1 = dma("gpsimd", bass.AP(dbc, do2, [[2048, 64], [1, 512]]),
                       bass.AP(drec, 64 * 2048 + doA,
                               [[2048, 1], [0, 64], [1, 512]]), dG,
                       waits=[t_d, dbc_free[oset]])
            t_b2 = dma("gpsimd",
                       bass.AP(dbc, do2 + 512, [[2048, 64], [1, 512]]),
                       bass.AP(drec, 64 * 2048 + doB,
                               [[2048, 1], [0, 64], [1, 512]]),
                       dG, waits=[t_d, dbc_free[oset]])
            drec_free[oset] = t_b2
            if NO_DIV[0]:
                t_nm = P.op("vector", lambda e, hp=hp, bA=bA, bB=bB: [
                    e.tensor_copy(ofm[0:64, hp * 512:(hp + 1) * 512],
                                  bank(bA, r0=0, r1=64)),
                    e.tensor_copy(stg[0:64, hp * 512:(hp + 1) * 512],
                                  bank(bB, r0=0, r1=64)),
                    e.drain(),
                ], waits=[t_pv, stg_free, ofm_free], inc=vv)
            else:
                t_nm = P.op("vector", lambda e, hp=hp, bA=bA, bB=bB, do2=do2: [
                    e.tensor_mul(ofm[0:64, hp * 512:(hp + 1) * 512],
                                 bank(bA, r0=0, r1=64),
                                 bass.AP(dbc, do2, [[2048, 64], [1, 512]])),
                    e.tensor_mul(stg[0:64, hp * 512:(hp + 1) * 512],
                                 bank(bB, r0=0, r1=64),
                                 bass.AP(dbc, do2 + 512, [[2048, 64], [1, 512]])),
                    e.drain(),
                ], waits=[t_b1, t_b2, stg_free, ofm_free], inc=vv)
            dbc_free[oset] = t_nm
            oset_free[oset] = t_nm
            stage_w.append(t_nm)
        qbuf_free[0] = t_pv
        t_sh = dma("gpsimd", ofm[64:128, :], stg[0:64, :], dG, waits=stage_w)

        if debug_y:
            dma("scalar", s_ofm[qb, :, :], ofm[:, :], dS, waits=[t_sh])
        t_ae = None
        for tp in range(2):
            t_mO = None
            for ecc in range(8):
                wb = ecc % 2
                t_w = dma("sync", wo[wb][:, :], s_wout[ecc, :, :], dW,
                          waits=[wo_free[wb]] + gO)
                for sub in range(4):
                    tc = tp * 2 + sub // 2
                    eoh = sub % 2
                    t_mO = P.op("tensor",
                                lambda e, ecc=ecc, wb=wb, tc=tc, eoh=eoh, sub=sub:
                                e.matmul(
                        bank(4 + sub),
                        ofm[:, ecc * 512 + tc * 128: ecc * 512 + (tc + 1) * 128],
                        wo[wb][:, eoh * 512:(eoh + 1) * 512],
                        start=(ecc == 0), stop=(ecc == 7)),
                        waits=([t_w] + ([t_sh, oset_free[0], oset_free[1],
                                         asb_free[0], asb_free[1], t_ae]
                                        if ecc == 0 else [])),
                        inc=(mm if sub == 3 else None), dr=(ecc == 7))
                wo_free[wb] = t_mO
            for sub2 in range(2):
                tc = tp * 2 + sub2
                t_ae = P.op("vector", lambda e, sub2=sub2, tc=tc: [
                    e.tensor_copy(asb[tc % 2][:, 0:512], bank(4 + sub2 * 2)),
                    e.tensor_copy(asb[tc % 2][:, 512:1024], bank(5 + sub2 * 2)),
                ], waits=[t_mO, asb_free[tc % 2]], inc=vv)
                t_l = layer_norm(asb[tc % 2][:, :], lna[:, :], waits=[t_ae])
                asb_free[tc % 2] = t_l
                row0 = 2048 + qb * 512 + tc * 128
                t_xr = dma("gpsimd", xres[:, :], x_in[row0:row0 + 128, :], dG,
                           waits=[xres_free])
                t_y = P.op("vector", lambda e: [
                    e.tensor_add(xres[:, :], xres[:, :], lna[:, :]), e.drain(),
                ], waits=[t_l, t_xr], inc=vv)
                t_sp = dma("scalar", s_y[qb * 4 + tc, :, :], xres[:, :], dS,
                           waits=[t_y])
                xres_free = t_sp
                y_spill[qb * 4 + tc] = t_sp
        oset_free[0] = t_ae
        oset_free[1] = t_ae
        stg_free = t_sh
        ofm_free = t_mO
        if qb + 1 < 4:
            t_q_next[0] = dma("sync", qbuf[0][:, :], s_q[qb + 1, :, :], dW,
                              waits=[q_spill[qb + 1], qbuf_free[0]])

    t_p2_v = (vv, vv.n)
    t_p2_s = (ss, ss.n)
    t_pc_all = dS.all_tags()
    barrier()

    t_wd = None
    for hc in range(24):
        t_wd = dma("sync", wd_sb[:, hc * 1024:(hc + 1) * 1024], s_wd[hc, :, :],
                   dW, waits=[t_pc_all] if hc == 0 else [])

    ybuf_free = [None] * 8
    x2f_free = None
    x2tm_free3 = [None, None]
    wgu_free = [None] * 4
    gu_free = {}
    f_free = {}
    trb3_free = None
    t_out = None

    for sbk in range(2):
        hbuf_w = []
        lns3 = [None] * 8
        th_gate = [None, None]
        gu_gens = []

        def gu_units(th, gate):
            for hc in range(24):
                wbg, wbu = (hc % 2) * 2, (hc % 2) * 2 + 1
                t_wg = dma("sync", wgu[wbg][:, :], s_wg[hc, :, :], dW,
                           waits=[wgu_free[wbg], t_pc_all])
                t_wu = dma("sync", wgu[wbu][:, :], s_wu[hc, :, :], dW,
                           waits=[wgu_free[wbu]])
                gb = (hc % 2) * 2
                ub = gb + 1
                t_mg = None
                for ecc in range(8):
                    t_mg = P.op("tensor",
                                lambda e, ecc=ecc, wbg=wbg, th=th, gb=gb:
                                e.matmul(
                        bank(gb), wgu[wbg][:, ecc * 128:(ecc + 1) * 128],
                        x2f[:, ecc * 1024 + th * 512:
                            ecc * 1024 + th * 512 + 512],
                        start=(ecc == 0), stop=(ecc == 7)),
                        waits=([t_wg, gate, gu_free.get(gb)]
                               if ecc == 0 else []),
                        inc=(mm if ecc == 7 else None))
                t_mu = None
                for ecc in range(8):
                    t_mu = P.op("tensor",
                                lambda e, ecc=ecc, wbu=wbu, th=th, ub=ub:
                                e.matmul(
                        bank(ub), wgu[wbu][:, ecc * 128:(ecc + 1) * 128],
                        x2f[:, ecc * 1024 + th * 512:
                            ecc * 1024 + th * 512 + 512],
                        start=(ecc == 0), stop=(ecc == 7)),
                        waits=([t_wu, gu_free.get(ub)] if ecc == 0 else []),
                        inc=(mm if ecc == 7 else None))
                wgu_free[wbg] = t_mg
                wgu_free[wbu] = t_mu
                t_si = P.op("scalar", lambda e, gb=gb: e.activation(
                    out=gt[:, :], in_=bank(gb), func=AF.Silu),
                    waits=[t_mg, hbuf_w[-1] if hbuf_w else None], inc=ss,
                    dr=True)
                t_h = P.op("vector", lambda e, hc=hc, th=th, ub=ub: e.tensor_mul(
                    hbuf[:, hc * 1024 + th * 512: hc * 1024 + th * 512 + 512],
                    gt[:, :], bank(ub)), waits=[t_si, t_mu], inc=vv, dr=True)
                gu_free[gb] = t_h
                gu_free[ub] = t_h
                hbuf_w.append(t_h)
                yield

        def gu_pump(n):
            while n > 0 and gu_gens:
                try:
                    next(gu_gens[0])
                    n -= 1
                except StopIteration:
                    gu_gens.pop(0)

        for step3 in range(10):
            t2 = step3 - 2
            if 0 <= t2 < 8:
                bx = t2 % 2
                t_ln = lns3[t2].cfin()
                t_tr = None
                for ecc in range(8):
                    t_tr = P.op("tensor", lambda e, ecc=ecc, bx=bx: e.transpose(
                        bank(6 + ecc // 4, c0=(ecc % 4) * 128,
                             c1=(ecc % 4) * 128 + 128),
                        x2tm[bx][:, ecc * 128:(ecc + 1) * 128], ident[:, :]),
                        waits=([t_ln, trb3_free] if ecc == 0 else []),
                        inc=(mm if ecc == 7 else None))
                x2tm_free3[bx] = t_tr
                t_ev = P.op("vector", lambda e, tt=t2: [
                    e.tensor_copy(
                        bass.AP(x2f, tt * 128,
                                [[8192, 128], [1024, 4], [1, 128]]),
                        bank(6, c1=512).rearrange("p (n c) -> p n c", n=4)),
                    e.tensor_copy(
                        bass.AP(x2f, 4096 + tt * 128,
                                [[8192, 128], [1024, 4], [1, 128]]),
                        bank(7, c1=512).rearrange("p (n c) -> p n c", n=4)),
                    e.drain(),
                ], waits=[t_tr], inc=vv)
                trb3_free = t_ev
                if t2 == 3:
                    th_gate[0] = (vv, vv.n)
                    gu_gens.append(gu_units(0, th_gate[0]))
                if t2 == 7:
                    th_gate[1] = (vv, vv.n)
                    gu_gens.append(gu_units(1, th_gate[1]))
            t1 = step3 - 1
            if 0 <= t1 < 8:
                lns3[t1].b()
            if step3 < 8:
                gtile = sbk * 8 + step3
                t_yl = dma("gpsimd", ybuf[step3][:, :], s_y[gtile, :, :], dG,
                           waits=[y_spill[gtile], ybuf_free[step3], t_p2_v,
                                  t_p2_s])
                bx = step3 % 2
                lns3[step3] = LN(ybuf[step3][:, :], x2tm[bx][:, :],
                                 waits=[t_yl],
                                 out_waits=[x2tm_free3[bx], x2f_free])
                lns3[step3].a()
            gu_pump(5)
        gu_pump(10000)
        t_hall = hbuf_w[-1]

        t_mf = None
        ln_prev = None
        for tc in range(8):
            for fh in range(2):
                pbx = 4 + fh
                for hc in range(24):
                    t_mf = P.op("tensor",
                                lambda e, hc=hc, tc=tc, fh=fh, pbx=pbx:
                                e.matmul(
                        bank(pbx),
                        hbuf[:, hc * 1024 + tc * 128: hc * 1024 + (tc + 1) * 128],
                        wd_sb[:, hc * 1024 + fh * 512: hc * 1024 + fh * 512 + 512],
                        start=(hc == 0), stop=(hc == 23)),
                        waits=([t_hall, t_wd, f_free.get(pbx)]
                               if hc == 0 else []),
                        inc=(mm if hc == 23 else None))
                f_free[pbx] = t_mf
            t_fe = P.op("vector", lambda e: [
                e.tensor_copy(fsb[:, 0:512], bank(4)),
                e.tensor_copy(fsb[:, 512:1024], bank(5)),
            ], waits=[t_mf, ln_prev], inc=vv)
            f_free[4] = t_fe
            f_free[5] = t_fe
            t_l = layer_norm(fsb[:, :], lnf[:, :], waits=[t_fe])
            ln_prev = t_l
            t_o = P.op("vector", lambda e, tt=tc: [
                e.tensor_add(outb[:, :], ybuf[tt][:, :], lnf[:, :]), e.drain(),
            ], waits=[t_l], inc=vv)
            ybuf_free[tc] = t_o
            gtile = sbk * 8 + tc
            t_out = dma("scalar", y_out[gtile * 128:(gtile + 1) * 128, :],
                        outb[:, :], dS, waits=[t_o])
        x2f_free = t_mf

    P.op("sync", lambda e: e.nop(),
         waits=[t_out] + dW.all_tags() + dG.all_tags() + dS.all_tags())

    all_sems = [mm, vv, ss] + dW.lanes + dS.lanes + dG.lanes
    import contextlib
    with contextlib.ExitStack() as stack:
        for s in all_sems:
            s.h = stack.enter_context(nc.semaphore("sem_" + s.name))
        block = stack.enter_context(nc.Block())

        @block.sync
        def _(e):
            P.emit("sync", e)

        @block.scalar
        def _(e):
            P.emit("scalar", e)

        @block.vector
        def _(e):
            P.emit("vector", e)

        @block.tensor
        def _(e):
            P.emit("tensor", e)

        @block.gpsimd
        def _(e):
            P.emit("gpsimd", e)

    return nc


_NC_CACHE = [None]


def _masks():
    m = np.zeros((128, 2048), dtype=np.float32)
    for pat in range(4):
        for p in range(128):
            c0 = pat * 128 + p
            m[p, pat * 512 + c0: pat * 512 + 512] = 1.0
    return m.astype(ml_dtypes.bfloat16)


def kernel(L0_attn_query, L0_attn_key, L0_attn_value, L0_attn_in_proj_weight,
           L0_attn_out_proj_weight, L0_ffn_w_gate, L0_ffn_w_up, L0_ffn_w_down):
    x = np.ascontiguousarray(np.asarray(L0_attn_query, dtype=np.float32))
    wqkv_t = np.ascontiguousarray(
        np.asarray(L0_attn_in_proj_weight, dtype=np.float32).T)
    wout_t = np.ascontiguousarray(
        np.asarray(L0_attn_out_proj_weight, dtype=np.float32).T)
    wg_t = np.ascontiguousarray(np.asarray(L0_ffn_w_gate, dtype=np.int32).T)
    wu_t = np.ascontiguousarray(np.asarray(L0_ffn_w_up, dtype=np.int32).T)
    wd_t = np.ascontiguousarray(np.asarray(L0_ffn_w_down, dtype=np.int32).T)
    masks = _masks()
    ident = np.eye(128, dtype=np.float32)
    kv1 = np.ones(T, dtype=ml_dtypes.bfloat16)
    kv0 = np.concatenate([np.zeros(TQ), np.ones(TQ)]).astype(ml_dtypes.bfloat16)

    if _NC_CACHE[0] is None:
        _NC_CACHE[0] = build_program()
    nc = _NC_CACHE[0]

    in_maps = []
    for c in range(8):
        b, r = c // 2, c % 2
        if r == 1:
            xl = x[b]
        else:
            xl = np.concatenate(
                [np.zeros((TQ, E), dtype=np.float32), x[b, :TQ]], axis=0)
        in_maps.append({
            "x_local": np.ascontiguousarray(xl),
            "wqkv_t": wqkv_t, "wout_t": wout_t,
            "wg_t": wg_t, "wu_t": wu_t, "wd_t": wd_t,
            "kvalid": kv1 if r == 1 else kv0,
            "masks": masks, "ident": ident,
        })
    res = run_bass_kernel_spmd(nc, in_maps, core_ids=list(range(8)))
    out = np.empty((4, T, E), dtype=np.float32)
    for c in range(8):
        b, r = c // 2, c % 2
        y = res.results[c]["y_out"]
        if r == 1:
            out[b, TQ:] = y
        else:
            out[b, :TQ] = y
    return out


